# version 04
# baseline (speedup 1.0000x reference)
import numpy as np
import ml_dtypes
from contextlib import ExitStack

import concourse.bass as bass
import concourse.mybir as mybir
from concourse.bass_utils import run_bass_kernel_spmd

F32 = mybir.dt.float32
BF16 = mybir.dt.bfloat16
AF = mybir.ActivationFunctionType
ALU = mybir.AluOpType
AX = mybir.AxisListType
BF = ml_dtypes.bfloat16

D = 1024
T = 2048
NB = 4
TC = NB * T
DK = 8
EPS = 1e-5
HMOE = 1408
HK = 11
HDIM = 64
HF = 4096
BATCH_BARRIER = False

ENGS = ["sync", "gpsimd", "tensor", "vector", "scalar"]


class K:

    def __init__(self, nc, stack):
        self.nc = nc
        self.stack = stack
        self.q = {e: [] for e in ENGS}
        self.h = {}
        self.c = {}
        for name in ["MM", "EA", "EV", "GP"]:
            self.sem(name)

    def sem(self, name):
        if name not in self.h:
            self.h[name] = self.stack.enter_context(self.nc.semaphore(name))
            self.c[name] = 0
        return name

    def op(self, eng, fn, waits=(), inc=None, incn=1, n_ins=1):
        v = None
        if eng == "vector":
            assert inc in (None, "EV"), inc
            self.c["EV"] += n_ins
            v = self.c["EV"]
        elif eng == "scalar":
            assert inc in (None, "EA"), inc
            self.c["EA"] += n_ins
            v = self.c["EA"]
        elif inc is not None:
            self.sem(inc)
            self.c[inc] += incn
            v = self.c[inc]
        ws = []
        for s, w in waits:
            if w:
                self.sem(s)
                ws.append((s, w))
        self.seq = getattr(self, "seq", 0) + 1
        self.q[eng].append((fn, ws, inc, incn, n_ins, self.seq))
        return v

    def truncate(self, max_seq):
        totals = {}
        for e in ENGS:
            kept = [o for o in self.q[e] if o[5] <= max_seq]
            self.q[e] = kept
            for fn, ws, inc, incn, n_ins, seq in kept:
                if e == "vector":
                    totals["EV"] = totals.get("EV", 0) + n_ins
                elif e == "scalar":
                    totals["EA"] = totals.get("EA", 0) + n_ins
                elif inc is not None:
                    totals[inc] = totals.get(inc, 0) + incn
        for s in self.c:
            self.c[s] = totals.get(s, 0)

    def emit(self, block):
        class Chain:

            def __init__(self, eng, sem):
                self._e = eng
                self._s = sem
                self.n = 0

            def __getattr__(self, a):
                f = getattr(self._e, a)
                if a.startswith("wait") or a in ("sem_inc", "drain"):
                    return f

                def call(*args, **kw):
                    if self.n > 0:
                        self._e.wait_ge(self._s, self.n)
                    ins = f(*args, **kw)
                    ins.then_inc(self._s, 1)
                    self.n += 1
                    return ins

                return call

        for ename in ENGS:
            ops = self.q[ename]
            h = self.h
            chain_sem = {"vector": "EV", "scalar": "EA"}.get(ename)

            def run(eng, ops=ops, chain_sem=chain_sem):
                lastw = {}
                ch = Chain(eng, h[chain_sem]) if chain_sem else None
                for fn, waits, inc, incn, n_ins, seq in ops:
                    for s, v in waits:
                        if v > lastw.get(s, 0):
                            eng.wait_ge(h[s], v)
                            lastw[s] = v
                    if ch is not None:
                        n0 = ch.n
                        fn(ch)
                        assert ch.n == n0 + n_ins, (
                            f"{ename} op emitted {ch.n - n0} instructions, declared {n_ins}")
                    else:
                        ins = fn(eng)
                        if inc is not None:
                            ins.then_inc(h[inc], incn)

            getattr(block, ename)(run)


def col(t, a, b):
    return t[:, a:b]


def stage_barrier(K_):
    snap = {s: v for s, v in K_.c.items() if v}
    for eng in ENGS:
        inc = "EV" if eng == "vector" else ("EA" if eng == "scalar" else None)
        K_.op(eng, lambda e: e.nop(), waits=list(snap.items()), inc=inc, n_ins=1)


class LNTail:
    def __init__(self, K_, g):
        self.K = K_
        self.g = g
        self.prev_pst_free = 0
        self.prev_apply = {}
        self.k_global = 0
        self.mm_stats = {}
        self.ev_out_last = 0
        self.last_ea_sq = [0] * DK
        self.ev_xtb_reader = {}

    def resid_rr(self, xin, out_cols, do, src_ap, src_waits):
        K_, g = self.K, self.g
        xtb = g["xtb"]
        k = self.k_global + do
        ld = K_.op("sync", lambda e, do=do, k=k:
                   e.dma_start(xtb[k % 2][:, :],
                               xin[do * 128:do * 128 + 128, out_cols[0]:out_cols[1]]),
                   waits=[("EV", self.ev_xtb_reader.get(k % 2, 0))], inc=f"LX{k % 2}", incn=16)
        ev = K_.op("vector",
                   lambda e, do=do, k=k, s=src_ap:
                   e.tensor_tensor(col(g["rrf"], do * 512, (do + 1) * 512),
                                   s, xtb[k % 2][:, :], ALU.add),
                   waits=list(src_waits) + [(f"LX{k % 2}", ld), ("EA", self.last_ea_sq[do])],
                   inc="EV")
        self.ev_xtb_reader[k % 2] = ev
        return ev

    def tile(self, xout, out_cols, ev_rr_list):
        K_, g = self.K, self.g
        rrf, rrb, sqb, lnrow = g["rrf"], g["rrb"], g["sqb"], g["lnrow"]
        onesb, onesf = g["onesb"], g["onesf"]
        PST, pab0, pab1 = g["PST"], g["pab0"], g["pab1"]
        t2b, obuf = g["t2b"], g["obuf"]
        MEAN, EX2, RSTD, BB = 0, 512, 1024, 1536
        ea_sq = {}
        ev_rrb = {}
        for do in range(DK):
            k = self.k_global + do
            rsl = col(rrf, do * 512, (do + 1) * 512)
            ev_rrb[do] = K_.op("vector", lambda e, d=rsl, o=rrb[k % 2]: e.tensor_copy(o[:, :], d),
                               waits=[("EV", ev_rr_list[do]), ("MM", self.mm_stats.get(k - 2, 0))],
                               inc="EV")
            ea_sq[do] = K_.op("scalar", lambda e, d=rsl, o=sqb[k % 2]: e.activation(o[:, :], d, AF.Square),
                              waits=[("EV", ev_rr_list[do]), ("MM", self.mm_stats.get(k - 2, 0))],
                              inc="EA")
            self.last_ea_sq[do] = ea_sq[do]

            def stats(e, d=do, k=k):
                e.matmul(PST[0:1, :], onesb[:], rrb[k % 2][:, :], start=(d == 0), stop=(d == DK - 1))
                return e.matmul(PST[32:33, :], onesb[:], sqb[k % 2][:, :], start=(d == 0), stop=(d == DK - 1))

            w = [("EV", ev_rrb[do]), ("EA", ea_sq[do])]
            if do == 0:
                w.append(("EV", self.prev_pst_free))
            self.mm_stats[k] = K_.op("tensor", stats, waits=w, inc="MM")
        mm_last_stats = self.mm_stats[self.k_global + DK - 1]

        def fin1(e):
            e.tensor_scalar(col(lnrow, MEAN, MEAN + 512), PST[0:1, :], 1.0 / D, None, ALU.mult)
            return e.tensor_scalar(col(lnrow, EX2, EX2 + 512), PST[32:33, :], 1.0 / D, None, ALU.mult)

        self.prev_pst_free = K_.op("vector", fin1, waits=[("MM", mm_last_stats)], inc="EV", n_ins=2)

        def fin2(e):
            e.tensor_tensor(col(lnrow, RSTD, RSTD + 512), col(lnrow, MEAN, MEAN + 512),
                            col(lnrow, MEAN, MEAN + 512), ALU.mult)
            return e.tensor_tensor(col(lnrow, EX2, EX2 + 512), col(lnrow, EX2, EX2 + 512),
                                   col(lnrow, RSTD, RSTD + 512), ALU.subtract)

        ev_var = K_.op("vector", fin2, inc="EV", n_ins=2)
        ea_sqrt = K_.op("scalar", lambda e: e.activation(col(lnrow, BB, BB + 512),
                                                         col(lnrow, EX2, EX2 + 512), AF.Sqrt,
                                                         bias=g["eps1"][0:1, 0:1]),
                        waits=[("EV", ev_var)], inc="EA")

        def fin3(e):
            e.reciprocal(col(lnrow, RSTD, RSTD + 512), col(lnrow, BB, BB + 512))
            e.tensor_scalar(col(lnrow, BB, BB + 512), col(lnrow, MEAN, MEAN + 512), -1.0, None, ALU.mult)
            return e.tensor_tensor(col(lnrow, BB, BB + 512), col(lnrow, BB, BB + 512),
                                   col(lnrow, RSTD, RSTD + 512), ALU.mult)

        ev_abrow = K_.op("vector", fin3, waits=[("EA", ea_sqrt)], inc="EV", n_ins=3)

        def abmm(e):
            e.matmul(pab0[:, 0:512], onesf[:], col(lnrow, RSTD, RSTD + 512), start=True, stop=True)
            return e.matmul(pab1[:, 0:512], onesf[:], col(lnrow, BB, BB + 512), start=True, stop=True)

        mm_ab = K_.op("tensor", abmm, waits=[("EV", ev_abrow), ("EV", self.ev_out_last)], inc="MM")

        st_out = {}
        for do in range(DK):
            k = self.k_global + do
            rsl = col(rrf, do * 512, (do + 1) * 512)

            def apply(e, d=rsl, ob=obuf[do % 2]):
                e.tensor_tensor(t2b[:, :], d, pab0[:, 0:512], ALU.mult)
                return e.tensor_tensor(ob[:, :], t2b[:, :], pab1[:, 0:512], ALU.add)

            w = [("MM", mm_ab)]
            if do >= 2:
                w.append((f"SO{do % 2}", st_out[do - 2]))
            else:
                w.append((f"SO{do % 2}", self.g.get(f"_ln_prev_st{do % 2}", 0)))
            ev_o = K_.op("vector", apply, waits=w, inc="EV", n_ins=2)
            st_out[do] = K_.op("sync",
                               lambda e, ob=obuf[do % 2], dst=xout[do * 128:do * 128 + 128,
                                                                   out_cols[0]:out_cols[1]]:
                               e.dma_start(dst, ob[:, :]),
                               waits=[("EV", ev_o)], inc=f"SO{do % 2}", incn=16)
            if do == DK - 1:
                self.ev_out_last = ev_o
        self.g["_ln_prev_st0"] = st_out[DK - 2]
        self.g["_ln_prev_st1"] = st_out[DK - 1]
        self.k_global += DK
        return st_out[DK - 1]


def r2d(v):
    return v.ap().rearrange("(a b) -> a b", b=1)


def emit_stage_a(nc, K_, stack, wd, xin, xout, hpad, g):
    vjh = nc.dram_tensor("vjh_s", [D, T], BF16, kind="Internal")
    sa = ExitStack()
    sb = lambda n, s, d=BF16: sa.enter_context(nc.sbuf_tensor(n, s, d))
    w_inb = sb("w_inb", [128, DK * 1024])
    xb = sb("xb", [128, DK * 2048])
    s0 = sb("s0", [128, 8 * 2048])
    Vs = sb("Vs", [128, 8 * 2048])
    VT = sb("VT", [128, 16 * 1024])
    gg = xb
    w_outb = VT
    toep = sb("toep", [128, 1152])
    ub = [sb(f"ub{i}", [128, 2052]) for i in range(2)]
    vtmp = sb("vtmp", [128, 2048])
    ybuf = [sb(f"ybuf{i}", [128, 512]) for i in range(2)]
    t1buf = sb("t1buf", [128, 512])
    b1t = sb("b1t", [64, 1], F32)
    b2t = sb("b2t", [64, 1], F32)
    w1b3 = sb("w1b3", [3, 64])
    z3 = sb("z3", [3, 1])
    v1b = sb("v1b", [64, 1])
    w2b = sb("w2b", [64, 64])
    v2b = sb("v2b", [64, 1])
    w3b = sb("w3b", [64, D])
    hcol = sb("hcol", [128, 8], F32)
    zcol = sb("zcol", [128, 1], F32)
    bin_sb = sb("bin_sb", [128, 16], F32)
    bcv_sb = sb("bcv_sb", [128, 16], F32)
    wc_sb = sb("wc_sb", [128, 48], F32)
    fb_sb = sb("fb_sb", [128, 8], F32)
    bout_sb = sb("bout_sb", [128, 8], F32)
    evb = [sb(f"evb{i}", [128, 512], F32) for i in range(2)]

    P = g["P"]
    PT = g["PT"]
    ident = g["ident"]
    ln = g["ln"]

    w1mlp = wd["L0_attn_w1"]
    w_in, b_in = wd["L0_attn_w_in"], wd["L0_attn_b_in"]
    w_conv, b_conv = wd["L0_attn_w_conv"], wd["L0_attn_b_conv"]
    b1, w2, b2, w3 = wd["L0_attn_b1"], wd["L0_attn_w2"], wd["L0_attn_b2"], wd["L0_attn_w3"]
    fbv, w_out, b_out = wd["L0_attn_filt_bias"], wd["L0_attn_w_out"], wd["L0_attn_b_out"]

    for dst, srcap in [(b1t[:], r2d(b1)), (b2t[:], r2d(b2))]:
        K_.op("sync", lambda e, d=dst, s=srcap: e.dma_start(d, s), inc="LDP", incn=16)
    for dst, srcap in [(w2b[:], w2[:, :]), (w3b[:], w3[:, :]), (w1b3[:], w1mlp[:, :])]:
        K_.op("gpsimd", lambda e, d=dst, s=srcap: e.dma_start(d, s), inc="GFW", incn=16)
    for sc_i in range(16):
        row0 = sc_i * 128 if sc_i < 8 else 2048 + (sc_i - 8) * 128
        K_.op("sync", lambda e, d=bin_sb[:, sc_i:sc_i + 1], s=r2d(b_in)[row0:row0 + 128, :]:
              e.dma_start(d, s), inc="LDP", incn=16)
        K_.op("sync", lambda e, d=bcv_sb[:, sc_i:sc_i + 1], s=r2d(b_conv)[row0:row0 + 128, :]:
              e.dma_start(d, s), inc="LDP", incn=16)
        K_.op("sync", lambda e, d=wc_sb[:, 3 * sc_i:3 * sc_i + 3],
              s=bass.AP(w_conv, 3 * row0, [[3, 128], [1, 3]]): e.dma_start(d, s), inc="LDP", incn=16)
    for cc in range(8):
        K_.op("sync", lambda e, d=fb_sb[:, cc:cc + 1], s=r2d(fbv)[cc * 128:cc * 128 + 128, :]:
              e.dma_start(d, s), inc="LDP", incn=16)
        K_.op("sync", lambda e, d=bout_sb[:, cc:cc + 1], s=r2d(b_out)[cc * 128:cc * 128 + 128, :]:
              e.dma_start(d, s), inc="LDP", incn=16)
    ld_prep = K_.c["LDP"]



    def miscmem(e):
        e.memset(zcol[:], 0.0)
        e.memset(ub[0][:, 0:2], 0.0)
        e.memset(ub[1][:, 0:2], 0.0)
        e.memset(z3[:, :], 0.0)
        return e.memset(vtmp[:, 0:8], 0.0)

    gl_fw = K_.c["GFW"]
    ev_fw = K_.op("vector", miscmem, waits=[("LDP", ld_prep)], inc="EV", n_ins=5)
    mm_p0 = K_.op("tensor", lambda e: e.matmul(P[1][0:64, 0:1], w1b3[0:3, :], z3[0:3, :],
                                               start=True, stop=True),
                  waits=[("EV", ev_fw), ("GFW", gl_fw)], inc="MM")
    ea_v1 = K_.op("scalar", lambda e: e.activation(v1b[:], P[1][0:64, 0:1], AF.Silu, bias=b1t[:]),
                  waits=[("LDP", ld_prep), ("MM", mm_p0)], inc="EA")
    mm_v2 = K_.op("tensor", lambda e: e.matmul(P[0][0:64, 0:1], w2b[:], v1b[:], start=True, stop=True),
                  waits=[("EV", ev_fw), ("EA", ea_v1), ("GFW", gl_fw)], inc="MM")
    ea_v2 = K_.op("scalar", lambda e: e.activation(v2b[0:64, :], P[0][0:64, 0:1], AF.Silu, bias=b2t[:]),
                  waits=[("MM", mm_v2)], inc="EA")

    def hmm(e):
        for cc in range(8):
            ins = e.matmul(P[1][:, cc:cc + 1], w3b[:, cc * 128:cc * 128 + 128], v2b[0:64, :],
                           start=True, stop=True)
        return ins

    mm_h = K_.op("tensor", hmm, waits=[("EA", ea_v2)], inc="MM")
    ev_h = K_.op("vector", lambda e: e.tensor_copy(hcol[:], P[1][:, 0:8]),
                 waits=[("MM", mm_h)], inc="EV")

    K_.op("sync", lambda e: e.dma_start(bass.AP(hpad, 0, [[1, 128], [1, 1]]), zcol[:]),
          waits=[("EV", ev_h)], inc="HPD", incn=16)
    K_.op("sync", lambda e: e.dma_start(bass.AP(hpad, 1152, [[1, 128], [1, 1]]), zcol[:]),
          inc="HPD", incn=16)
    for cc in range(8):
        K_.op("sync", lambda e, c=cc: e.dma_start(bass.AP(hpad, 128 + c * 128, [[1, 128], [1, 1]]),
                                                  hcol[:, c:c + 1]), inc="HPD", incn=16)
    ld_hpad = K_.c["HPD"]
    for p in range(128):
        K_.op("gpsimd", lambda e, p=p: e.dma_start(toep[p:p + 1, 0:1152],
                                                   bass.AP(hpad, 128 - p, [[1, 1], [1, 1152]])),
              waits=[("HPD", ld_hpad), ("EV", ev_fw)], inc="GTP", incn=16)
    gl_toep = K_.c["GTP"]

    print("seq@prep_done:", K_.seq)
    ea_uev = {}
    ev_conv = {}
    prev_mm_after_u = 0
    vs_prev = 0
    prev_mm_stream = [0, 0]
    for b in range(NB):
        if b > 0 and BATCH_BARRIER:
            stage_barrier(K_)
        gl_xb = None
        for dk in range(DK):
            gl_xb = K_.op("gpsimd", lambda e, d=col(xb, dk * 2048, (dk + 1) * 2048),
                          s=xin[dk * 128:dk * 128 + 128, b * T:(b + 1) * T]: e.dma_start(d, s),
                          waits=[("MM", prev_mm_after_u)], inc="XB", incn=16)

        gl_win = [None, None]
        for dk in range(DK):
            gl_win[0] = K_.op("gpsimd", lambda e, d=col(w_inb, dk * 1024, (dk + 1) * 1024),
                              s=w_in[dk * 128:dk * 128 + 128, 0:1024]: e.dma_start(d, s),
                              waits=[("MM", prev_mm_stream[1])], inc="GWI", incn=16)

        mm_tp = {}
        ev_tpcopy = {}
        ev_V = {}
        for sc_i in range(16):
            s_half = 0 if sc_i < 8 else 1
            cc = sc_i % 8
            if sc_i == 8:
                for dk in range(DK):
                    gl_win[1] = K_.op("gpsimd", lambda e, d=col(w_inb, dk * 1024, (dk + 1) * 1024),
                                      s=w_in[dk * 128:dk * 128 + 128, 2048:3072]: e.dma_start(d, s),
                                      waits=[("MM", prev_mm_stream[0])], inc="GWI", incn=16)
            for tt in range(4):
                idx = sc_i * 4 + tt

                def ummgroup(e, cc=cc, tt=tt, idx=idx):
                    bank = P[idx % 2]
                    for dk in range(DK):
                        ins = e.matmul(bank[:, 0:512],
                                       col(w_inb, dk * 1024 + cc * 128,
                                           dk * 1024 + cc * 128 + 128),
                                       col(xb, dk * 2048 + tt * 512, dk * 2048 + tt * 512 + 512),
                                       start=(dk == 0), stop=(dk == DK - 1))
                    return ins

                w = [("XB", gl_xb), ("GWI", gl_win[s_half])]
                if idx >= 2:
                    w.append(("EA", ea_uev[(b, idx - 2)]))
                if b == 0 and idx < 2:
                    w += [("EA", ea_v2), ("EV", ev_h)]
                if b > 0 and idx < 2:
                    w.append(("EV", ln.ev_out_last))
                mm_u = K_.op("tensor", ummgroup, waits=w, inc="MM")
                if tt == 3 and cc == 7:
                    prev_mm_stream[s_half] = mm_u

                wv = [("MM", mm_u)]
                if sc_i >= 2 and tt == 0:
                    wv.append(("EV", ev_conv[(b, sc_i - 2)]))
                elif b > 0 and sc_i < 2 and tt == 0:
                    wv.append(("EV", ev_conv[(b - 1, sc_i + 14)]))
                ea_uev[(b, idx)] = K_.op(
                    "scalar",
                    lambda e, sc_i=sc_i, tt=tt, idx=idx:
                    e.activation(ub[sc_i % 2][:, 2 + tt * 512: 2 + (tt + 1) * 512],
                                 P[idx % 2][:, 0:512], AF.Identity, bias=bin_sb[:, sc_i:sc_i + 1]),
                    waits=wv, inc="EA")

            dst0 = col(s0, cc * 2048, (cc + 1) * 2048) if sc_i < 8 else vtmp[:, :]

            def conv3(e, u=ub[sc_i % 2], dst0=dst0, sc_i=sc_i):
                e.tensor_scalar(dst0, u[:, 0:2048], wc_sb[:, 3 * sc_i:3 * sc_i + 1],
                                bcv_sb[:, sc_i:sc_i + 1], ALU.mult, ALU.add)
                e.scalar_tensor_tensor(dst0, u[:, 1:2049], wc_sb[:, 3 * sc_i + 1:3 * sc_i + 2],
                                       dst0, ALU.mult, ALU.add)
                return e.scalar_tensor_tensor(dst0, u[:, 2:2050], wc_sb[:, 3 * sc_i + 2:3 * sc_i + 3],
                                              dst0, ALU.mult, ALU.add)

            ev_conv[(b, sc_i)] = K_.op("vector", conv3, waits=[("EA", ea_uev[(b, sc_i * 4 + 3)])],
                                       inc="EV", n_ins=3)

            if sc_i >= 8:
                def vp(e, cc=cc):
                    return e.tensor_tensor(col(Vs, cc * 2048, (cc + 1) * 2048), vtmp[:, :],
                                           col(s0, cc * 2048, (cc + 1) * 2048), ALU.mult)

                w = []
                if b > 0 and cc == 0:
                    w.append(("VS", vs_prev))
                ev_V[cc] = K_.op("vector", vp, waits=w, inc="EV")
                vs_last = K_.op("sync", lambda e, cc=cc: e.dma_start(
                    vjh[cc * 128:(cc + 1) * 128, :], col(Vs, cc * 2048, (cc + 1) * 2048)),
                    waits=[("EV", ev_V[cc])], inc="VS", incn=16)
        print(f"seq@u_done b={b}:", K_.seq)
        vs_prev = vs_last
        vtl_last = None
        for tc in range(16):
            w = [("VS", vs_last)]
            if b > 0:
                w.append(("MM", prev_mm_after_u))
            vtl_last = K_.op("sync", lambda e, tc=tc: e.dma_start(
                col(VT, tc * 1024, (tc + 1) * 1024),
                vjh[:, tc * 128:(tc + 1) * 128], transpose=True),
                waits=w, inc="VTL", incn=16)
        prev_mm_after_u = K_.c["MM"]

        ea_yb = {}
        ev_g = {}
        for cc in range(8):
            for j0 in range(4):
                idx = cc * 4 + j0
                tis = list(range(max(0, 4 * j0 - 8), min(15, 4 * j0 + 3) + 1))
                tis.remove(4 * j0)
                tis = [4 * j0] + tis

                def cvmm(e, cc=cc, j0=j0, idx=idx, tis=tis):
                    bank = P[2 + idx % 2]
                    for n_i, ti in enumerate(tis):
                        jlo = max(0, ti - 4 * j0)
                        jhi = min(3, ti - 4 * j0 + 8)
                        dlo = 4 * j0 + jlo - ti
                        ins = e.matmul(bank[:, jlo * 128:(jhi + 1) * 128],
                                       col(VT, ti * 1024 + cc * 128, ti * 1024 + (cc + 1) * 128),
                                       col(toep, 128 * dlo, 128 * (dlo + jhi - jlo + 1)),
                                       start=(n_i == 0), stop=(n_i == len(tis) - 1))
                    return ins

                w = [("VTL", vtl_last)]
                if b == 0 and idx == 0:
                    w.append(("GTP", gl_toep))
                if idx >= 2:
                    w.append(("EA", ea_yb[idx - 2]))
                mm_cv = K_.op("tensor", cvmm, waits=w, inc="MM")

                w = [("MM", mm_cv)]
                if idx >= 2:
                    w.append(("EV", ev_g[idx - 2]))
                ea_yb[idx] = K_.op("scalar",
                                   lambda e, idx=idx: e.activation(ybuf[idx % 2][:, :],
                                                                   P[2 + idx % 2][:, 0:512], AF.Identity),
                                   waits=w, inc="EA")

                def gev(e, cc=cc, j0=j0, idx=idx):
                    e.scalar_tensor_tensor(t1buf[:, :],
                                           col(Vs, cc * 2048 + j0 * 512, cc * 2048 + (j0 + 1) * 512),
                                           fb_sb[:, cc:cc + 1], ybuf[idx % 2][:, :],
                                           ALU.mult, ALU.add)
                    return e.tensor_tensor(col(gg, cc * 2048 + j0 * 512, cc * 2048 + (j0 + 1) * 512),
                                           t1buf[:, :],
                                           col(s0, cc * 2048 + j0 * 512, cc * 2048 + (j0 + 1) * 512),
                                           ALU.mult)

                ev_g[idx] = K_.op("vector", gev, waits=[("EA", ea_yb[idx])], inc="EV", n_ins=2)

        print(f"seq@longconv_done b={b}:", K_.seq)
        mm_cv_last = mm_cv
        gl_wout = None
        for cc in range(8):
            gl_wout = K_.op("gpsimd", lambda e, cc=cc: e.dma_start(
                col(w_outb, cc * 1024, cc * 1024 + 1024),
                w_out[cc * 128:cc * 128 + 128, 0:1024]),
                waits=[("MM", mm_cv_last)], inc="GWO", incn=16)

        ea_rv = {}
        ev_rr = {}
        for tt in range(4):
            cols = (b * T + tt * 512, b * T + tt * 512 + 512)
            ev_rr_list = []
            for do in range(DK):
                k = tt * 8 + do

                def wmm(e, tt=tt, do=do, k=k):
                    bank = P[4 + k % 2]
                    for cc in range(8):
                        ins = e.matmul(bank[:, 0:512],
                                       col(w_outb, cc * 1024 + do * 128, cc * 1024 + do * 128 + 128),
                                       col(gg, cc * 2048 + tt * 512, cc * 2048 + (tt + 1) * 512),
                                       start=(cc == 0), stop=(cc == 7))
                    return ins

                w = [("EV", ev_g[7 * 4 + tt]), ("GWO", gl_wout)]
                if k >= 2:
                    w.append(("EA", ea_rv[k - 2]))
                mm_w = K_.op("tensor", wmm, waits=w, inc="MM")

                w = [("MM", mm_w), ("EV", ev_rr.get(k - 2, 0))]
                ea_rv[k] = K_.op("scalar",
                                 lambda e, k=k, do=do: e.activation(evb[k % 2][:, :],
                                                                    P[4 + k % 2][:, 0:512], AF.Identity,
                                                                    bias=bout_sb[:, do:do + 1]),
                                 waits=w, inc="EA")
                ev_rr[k] = ln.resid_rr(xin, cols, do, evb[k % 2][:, :], [("EA", ea_rv[k])])
                ev_rr_list.append(ev_rr[k])
            ln.tile(xout, cols, ev_rr_list)
        print(f"seq@batch_done b={b}:", K_.seq)
        prev_mm_after_u = K_.c["MM"]

    sa.close()



def emit_stage_b(nc, K_, stack, wd, xin, xout, g):
    sb_ = ExitStack()
    sb = lambda n, s, d=BF16: sb_.enter_context(nc.sbuf_tensor(n, s, d))
    NE = 9
    xfb = sb("xfb", [128, DK * 1024])
    y_sb = sb("y_sb", [128, DK * 1024], F32)
    w1b = sb("w1b", [128, DK * HMOE])
    w3b = sb("w3b", [128, DK * HMOE])
    w2b = sb("w2b", [128, HK * 1024])
    h_sc = sb("h_sc", [128, 2 * HK * 512])
    bb_t = [sb(f"bb_t{i}", [128, 512], F32) for i in range(2)]
    t_f32 = [sb(f"t_f32{i}", [128, 512], F32) for i in range(2)]
    h_f32 = [sb(f"h_f32{i}", [128, 512], F32) for i in range(2)]
    probs = sb("probs", [128, 64], F32)
    scores = sb("scores", [128, 64], F32)
    biased = sb("biased", [128, 64], F32)
    acc = sb("acc", [128, 64], F32)
    tmp8 = sb("tmp8", [128, 64], F32)
    rw = sb("rw", [128, 64], F32)
    ssum = sb("ssum", [128, 8], F32)
    rsum = sb("rsum", [128, 8], F32)
    rwD = sb("rwD", [8, 1024], F32)
    rwDb = sb("rwDb", [8, 1024])
    gbt = sb("gbt", [128, 64], F32)
    gwTb = sb("gwTb", [128, 64])
    ltbc = sb("ltbc", [128, 64], F32)
    idf = sb("idf", [128, 128], F32)
    esel = sb("esel", [8, 1024], F32)
    b1t = sb("b1t", [128, NE * HK], F32)
    b3t = sb("b3t", [128, NE * HK], F32)
    b2b = sb("b2b", [8, 1024])
    sb2t = sb("sb2t", [128, 8], F32)
    gbrow = sb("gbrow", [1, 8], F32)

    P = g["P"]
    ln = g["ln"]
    w1d, b1d = wd["L0_ffn_w1"], wd["L0_ffn_b1"]
    w3d, b3d = wd["L0_ffn_w3"], wd["L0_ffn_b3"]
    w2d, b2d = wd["L0_ffn_w2"], wd["L0_ffn_b2"]
    sw1, sb1 = wd["L0_ffn_sw1"], wd["L0_ffn_sb1"]
    sw3, sb3 = wd["L0_ffn_sw3"], wd["L0_ffn_sb3"]
    sw2, sb2 = wd["L0_ffn_sw2"], wd["L0_ffn_sb2"]
    gwT = wd["gate_wT"]
    gb = wd["L0_ffn_gate_b"]
    ltin = wd["lt_bc_in"]
    idfin = wd["identf_in"]
    eselin = wd["esel_in"]

    for e in range(8):
        for hk in range(HK):
            off = e * HMOE + hk * 128
            K_.op("sync", lambda e_, d=b1t[:, (e * HK + hk):(e * HK + hk) + 1],
                  s=bass.AP(b1d, off, [[1, 128], [1, 1]]): e_.dma_start(d, s), inc="LDB", incn=16)
            K_.op("sync", lambda e_, d=b3t[:, (e * HK + hk):(e * HK + hk) + 1],
                  s=bass.AP(b3d, off, [[1, 128], [1, 1]]): e_.dma_start(d, s), inc="LDB", incn=16)
    for hk in range(HK):
        K_.op("sync", lambda e_, d=b1t[:, (8 * HK + hk):(8 * HK + hk) + 1],
              s=bass.AP(sb1, hk * 128, [[1, 128], [1, 1]]): e_.dma_start(d, s), inc="LDB", incn=16)
        K_.op("sync", lambda e_, d=b3t[:, (8 * HK + hk):(8 * HK + hk) + 1],
              s=bass.AP(sb3, hk * 128, [[1, 128], [1, 1]]): e_.dma_start(d, s), inc="LDB", incn=16)
    for do in range(8):
        K_.op("sync", lambda e_, d=sb2t[:, do:do + 1],
              s=bass.AP(sb2, do * 128, [[1, 128], [1, 1]]): e_.dma_start(d, s), inc="LDB", incn=16)
    K_.op("sync", lambda e_: e_.dma_start(gbrow[:], bass.AP(gb, 0, [[8, 1], [1, 8]])), inc="LDB", incn=16)
    K_.op("sync", lambda e_: e_.dma_start(ltbc[:], ltin[:, :]), inc="LDB", incn=16)
    K_.op("sync", lambda e_: e_.dma_start(idf[:], idfin[:, :]), inc="LDB", incn=16)
    K_.op("sync", lambda e_: e_.dma_start(esel[:], eselin[:, :]), inc="LDB", incn=16)
    ld_b = K_.c["LDB"]
    K_.op("gpsimd", lambda e_: e_.dma_start(b2b[:], b2d[:, :]), inc="GBC", incn=16)
    for dk in range(DK):
        K_.op("gpsimd", lambda e_, d=gwTb[:, dk * 8:(dk + 1) * 8],
              s=gwT[dk * 128:(dk + 1) * 128, 0:8]: e_.dma_start(d, s), inc="GBC", incn=16)
    gl_bc = K_.c["GBC"]

    mm_gb = K_.op("tensor", lambda e: e.matmul(P[4][:, 0:8], g["onesf"][:], gbrow[:],
                                               start=True, stop=True),
                  waits=[("LDB", ld_b), ("EV", g["ev_init"])], inc="MM")

    def gbt_cp(e):
        for gi in range(8):
            ins = e.tensor_copy(col(gbt, gi * 8, gi * 8 + 8), P[4][:, 0:8])
        return ins

    ev_gbt = K_.op("vector", gbt_cp, waits=[("MM", mm_gb)], inc="EV", n_ins=8)

    mm_prev_st = 0
    ea_exp_prev = 0
    bank_free = {}
    ev_hsc_w = {}
    gwc = {"GW1": 0, "GW3": 0, "GW2": 0}
    mm_h_last_prev = 0
    mm_y_last_prev = 0
    ev_y_last = 0
    ev_rwb_read = 0
    ea_logit_prev = 0

    for st in range(TC // 1024):
        c0 = st * 1024
        gl_xf = None
        for dk in range(DK):
            gl_xf = K_.op("gpsimd", lambda e, d=col(xfb, dk * 1024, (dk + 1) * 1024),
                          s=xin[dk * 128:(dk + 1) * 128, c0:c0 + 1024]: e.dma_start(d, s),
                          waits=[("MM", mm_prev_st)], inc="XF", incn=16)

        ea_exp = {}
        for gi in range(8):
            def lmm(e, gi=gi):
                for dk in range(DK):
                    ins = e.matmul(P[4][:, 0:8],
                                   col(xfb, dk * 1024 + gi * 128, dk * 1024 + gi * 128 + 128),
                                   gwTb[:, dk * 8:(dk + 1) * 8],
                                   start=(dk == 0), stop=(dk == DK - 1))
                return ins

            w = [("XF", gl_xf), ("GBC", gl_bc)]
            w.append(("EA", ea_exp[gi - 1] if gi else ea_exp_prev))
            if gi == 0:
                w.append(("EV", ev_y_last))
                if st == 0:
                    w.append(("EV", ev_gbt))
            mm_l = K_.op("tensor", lmm, waits=w, inc="MM")
            ea_exp[gi] = K_.op("scalar",
                               lambda e, gi=gi: e.activation(col(probs, gi * 8, gi * 8 + 8),
                                                             P[4][:, 0:8], AF.Exp),
                               waits=[("MM", mm_l)], inc="EA")
        ea_exp_prev = ea_exp[7]

        def seg(e):
            e.reduce_sum(ssum[:, :], probs[:, :].rearrange("p (g e) -> p g e", e=8),
                         axis=AX.X)
            return e.reciprocal(rsum[:, :], ssum[:, :])

        K_.op("vector", seg, waits=[("EA", ea_exp[7]), ("EV", ev_gbt)], inc="EV", n_ins=2)

        def scr(e):
            for gi in range(8):
                e.tensor_scalar(col(scores, gi * 8, gi * 8 + 8), col(probs, gi * 8, gi * 8 + 8),
                                rsum[:, gi:gi + 1], None, ALU.mult)
            return e.tensor_tensor(biased[:, :], scores[:, :], gbt[:, :], ALU.add)

        K_.op("vector", scr, inc="EV", n_ins=9)

        def rank(e):
            for gi in range(8):
                bg = col(biased, gi * 8, gi * 8 + 8)
                ag = col(acc, gi * 8, gi * 8 + 8)
                tg = col(tmp8, gi * 8, gi * 8 + 8)
                for j in range(8):
                    bj = biased[:, gi * 8 + j:gi * 8 + j + 1]
                    if j == 0:
                        e.tensor_scalar(ag, bg, bj, None, ALU.is_lt)
                    else:
                        e.scalar_tensor_tensor(ag, bg, bj, ag, ALU.is_lt, ALU.add)
                    e.scalar_tensor_tensor(tg, bg, bj, col(ltbc, j * 8, j * 8 + 8),
                                           ALU.is_equal, ALU.mult)
                    e.tensor_tensor(ag, ag, tg, ALU.add)
            e.tensor_scalar(tmp8[:, :], acc[:, :], 2.0, None, ALU.is_lt)
            return e.tensor_tensor(rw[:, :], tmp8[:, :], scores[:, :], ALU.mult)

        ev_rw = K_.op("vector", rank, inc="EV", n_ins=8 * 24 + 2)

        ev_cp = 0
        for gi in range(8):
            w = [("EV", ev_rw), ("LDB", ld_b)]
            if gi:
                w.append(("EV", ev_cp))
            else:
                w.append(("EV", ev_rwb_read))
            mm_t = K_.op("tensor", lambda e, gi=gi: e.transpose(P[5][0:8, 0:128],
                                                                col(rw, gi * 8, gi * 8 + 8), idf[:]),
                         waits=w, inc="MM")
            ev_cp = K_.op("vector", lambda e, gi=gi: e.tensor_copy(rwD[0:8, gi * 128:(gi + 1) * 128],
                                                                   P[5][0:8, 0:128]),
                          waits=[("MM", mm_t)], inc="EV")
        ev_rwD = K_.op("vector", lambda e: e.tensor_copy(rwDb[:, :], rwD[:, :]), inc="EV")

        for e_i in range(NE):
            if e_i < 8:
                s1ap = bass.AP(w1d, e_i * D * HMOE, [[HMOE, 128], [128 * HMOE, DK], [1, HMOE]])
                s3ap = bass.AP(w3d, e_i * D * HMOE, [[HMOE, 128], [128 * HMOE, DK], [1, HMOE]])
                s2ap = bass.AP(w2d, e_i * HMOE * D, [[1024, 128], [128 * 1024, HK], [1, 1024]])
            else:
                s1ap = bass.AP(sw1, 0, [[HMOE, 128], [128 * HMOE, DK], [1, HMOE]])
                s3ap = bass.AP(sw3, 0, [[HMOE, 128], [128 * HMOE, DK], [1, HMOE]])
                s2ap = bass.AP(sw2, 0, [[1024, 128], [128 * 1024, HK], [1, 1024]])
            gwc["GW1"] = K_.op("gpsimd", lambda e, s=s1ap:
                               e.dma_start(w1b[:, :].rearrange("p (dk h) -> p dk h", h=HMOE), s),
                               waits=[("MM", mm_h_last_prev)], inc="GW1", incn=16)
            gwc["GW3"] = K_.op("gpsimd", lambda e, s=s3ap:
                               e.dma_start(w3b[:, :].rearrange("p (dk h) -> p dk h", h=HMOE), s),
                               waits=[("MM", mm_h_last_prev)], inc="GW3", incn=16)
            gwc["GW2"] = K_.op("gpsimd", lambda e, s=s2ap:
                               e.dma_start(w2b[:, :].rearrange("p (hk m) -> p hk m", m=1024), s),
                               waits=[("MM", mm_y_last_prev)], inc="GW2", incn=16)

            bcol = e_i * HK
            mm_h_last = None
            for tt in range(2):
                if e_i < 8:
                    w = [("EV", ev_rwD), ("EV", ev_rwb_read)]
                    mm_rwb = K_.op("tensor",
                                   lambda e, e_i=e_i, tt=tt:
                                   e.matmul(P[5][:, 0:512], esel[:, e_i * 128:(e_i + 1) * 128],
                                            rwD[0:8, tt * 512:(tt + 1) * 512], start=True, stop=True),
                                   waits=w, inc="MM")
                for hk in range(HK):
                    pa = P[(hk % 2) * 2]
                    pb = P[(hk % 2) * 2 + 1]
                    par = hk % 2

                    def hmm(e, e_i=e_i, tt=tt, hk=hk, pa=pa, pb=pb):
                        for dk in range(DK):
                            e.matmul(pa[:, 0:512],
                                     col(w1b, dk * HMOE + hk * 128, dk * HMOE + hk * 128 + 128),
                                     col(xfb, dk * 1024 + tt * 512, dk * 1024 + tt * 512 + 512),
                                     start=(dk == 0), stop=(dk == DK - 1))
                        for dk in range(DK):
                            ins = e.matmul(pb[:, 0:512],
                                           col(w3b, dk * HMOE + hk * 128, dk * HMOE + hk * 128 + 128),
                                           col(xfb, dk * 1024 + tt * 512, dk * 1024 + tt * 512 + 512),
                                           start=(dk == 0), stop=(dk == DK - 1))
                        return ins

                    w = [("GW1", gwc["GW1"]), ("GW3", gwc["GW3"]), ("XF", gl_xf),
                         ("EV", bank_free.get((hk % 2) * 2, 0)),
                         ("EA", bank_free.get("b" + str((hk % 2) * 2 + 1), 0))]
                    if e_i < 8 and hk == 0:
                        w.append(("MM", mm_rwb))
                    mm_h = K_.op("tensor", hmm, waits=w, inc="MM")
                    mm_h_last = mm_h

                    ea_bsb = K_.op("scalar",
                                   lambda e, pb=pb, par=par, bcol=bcol, hk=hk:
                                   e.activation(bb_t[par][:, :], pb[:, 0:512], AF.Identity,
                                                bias=b3t[:, bcol + hk:bcol + hk + 1]),
                                   waits=[("MM", mm_h), ("EV", bank_free.get("t" + str(par), 0))],
                                   inc="EA")
                    bank_free["b" + str((hk % 2) * 2 + 1)] = ea_bsb
                    ev_t = K_.op("vector",
                                 lambda e, pa=pa, par=par, bcol=bcol, hk=hk:
                                 e.scalar_tensor_tensor(t_f32[par][:, :], pa[:, 0:512],
                                                        b1t[:, bcol + hk:bcol + hk + 1],
                                                        bb_t[par][:, :], ALU.add, ALU.mult),
                                 waits=[("EA", ea_bsb), ("EA", bank_free.get("t" + str(par), 0))],
                                 inc="EV")
                    bank_free[(hk % 2) * 2] = ev_t
                    hslice = col(h_sc, (tt * HK + hk) * 512, (tt * HK + hk) * 512 + 512)
                    if e_i < 8:
                        ea_si = K_.op("scalar",
                                      lambda e, par=par: e.activation(h_f32[par][:, :],
                                                                      t_f32[par][:, :], AF.Silu),
                                      waits=[("EV", ev_t), ("EV", ev_hsc_w.get("f" + str(par), 0)),
                                             ("MM", mm_y_last_prev)],
                                      inc="EA")
                        bank_free["t" + str(par)] = ea_si
                        ev_hs = K_.op("vector",
                                      lambda e, par=par, hs=hslice:
                                      e.tensor_tensor(hs, h_f32[par][:, :], P[5][:, 0:512], ALU.mult),
                                      waits=[("EA", ea_si)], inc="EV")
                        ev_hsc_w["f" + str(par)] = ev_hs
                        ev_rwb_read = ev_hs
                    else:
                        ea_si = K_.op("scalar",
                                      lambda e, par=par, hs=hslice:
                                      e.activation(hs, t_f32[par][:, :], AF.Silu),
                                      waits=[("EV", ev_t), ("MM", mm_y_last_prev)],
                                      inc="EA")
                        bank_free["t" + str(par)] = ea_si
                        ev_hs = ea_si
                    ev_hsc_w[(tt, hk)] = (ev_hs, e_i >= 8)

            mm_h_last_prev = mm_h_last

            mm_y_last = None
            for tt in range(2):
                for do in range(DK):
                    def ymm(e, e_i=e_i, tt=tt, do=do):
                        for hk in range(HK):
                            ins = e.matmul(P[4][:, 0:512],
                                           col(w2b, hk * 1024 + do * 128, hk * 1024 + do * 128 + 128),
                                           col(h_sc, (tt * HK + hk) * 512, (tt * HK + hk) * 512 + 512),
                                           start=(hk == 0), stop=(hk == HK - 1 and e_i < 8))
                        if e_i == 8:
                            ins = e.matmul(P[4][:, 0:512], b2b[0:8, do * 128:(do + 1) * 128],
                                           rwDb[0:8, tt * 512:(tt + 1) * 512], start=False, stop=True)
                        return ins

                    w = [("GW2", gwc["GW2"]), ("EV", ev_y_last)]
                    hs_ev, is_sc = ev_hsc_w[(tt, HK - 1)]
                    if is_sc:
                        w.append(("EA", hs_ev))
                    else:
                        w.append(("EV", hs_ev))
                    if e_i == 8:
                        w.append(("EV", ev_rwD))
                    mm_y = K_.op("tensor", ymm, waits=w, inc="MM")
                    mm_y_last = mm_y

                    ysl = col(y_sb, do * 1024 + tt * 512, do * 1024 + tt * 512 + 512)
                    if e_i == 0:
                        ev_y = K_.op("vector", lambda e, ysl=ysl: e.tensor_copy(ysl, P[4][:, 0:512]),
                                     waits=[("MM", mm_y)], inc="EV")
                    elif e_i < 8:
                        ev_y = K_.op("vector", lambda e, ysl=ysl:
                                     e.tensor_tensor(ysl, ysl, P[4][:, 0:512], ALU.add),
                                     waits=[("MM", mm_y)], inc="EV")
                    else:
                        ev_y = K_.op("vector", lambda e, ysl=ysl, do=do:
                                     e.scalar_tensor_tensor(ysl, P[4][:, 0:512], sb2t[:, do:do + 1],
                                                            ysl, ALU.add, ALU.add),
                                     waits=[("MM", mm_y)], inc="EV")
                    ev_y_last = ev_y
            mm_y_last_prev = mm_y_last

        for tt in range(2):
            cols = (c0 + tt * 512, c0 + tt * 512 + 512)
            ev_rr_list = []
            for do in range(DK):
                ysl = col(y_sb, do * 1024 + tt * 512, do * 1024 + tt * 512 + 512)
                ev_rr_list.append(ln.resid_rr(xin, cols, do, ysl, [("EV", ev_y_last)]))
            ln.tile(xout, cols, ev_rr_list)

        mm_prev_st = K_.c["MM"]
        print(f"seq@B_st{st}:", K_.seq)

    sb_.close()


def emit_stage_c(nc, K_, stack, wd, xin, xout, g):
    sc_ = ExitStack()
    sb = lambda n, s, d=BF16: sc_.enter_context(nc.sbuf_tensor(n, s, d))
    xa = sb("xa", [128, DK * 2048])
    qoT = sb("qoT", [128, DK * 2048])
    kT = sb("kT", [128, 4 * 2048])
    vext = sb("vext", [128, 4 * 16 * 65])
    Wqb = sb("Wqb", [128, DK * 1024])
    Wkvb = sb("Wkvb", [128, DK * 512])
    Wob = sb("Wob", [128, DK * 1024])
    alib = sb("alib", [128, 256], F32)
    es = [sb(f"es{i}", [128, 512]) for i in range(2)]
    o_sb = sb("o_sb", [128, 512], F32)
    rlrow = sb("rlrow", [1, 512], F32)
    evb = [sb(f"evbc{i}", [128, 512], F32) for i in range(2)]

    P = g["P"]
    ln = g["ln"]
    onesf = g["onesf"]
    Wq, Wkv, Wo = wd["L1_attn_Wq"], wd["L1_attn_Wkv"], wd["L1_attn_Wo"]
    alin = wd["alibi_in"]

    for dk in range(DK):
        K_.op("gpsimd", lambda e, d=col(Wqb, dk * 1024, (dk + 1) * 1024),
              s=Wq[dk * 128:(dk + 1) * 128, :]: e.dma_start(d, s), inc="GWQ", incn=16)
        K_.op("gpsimd", lambda e, d=col(Wkvb, dk * 512, (dk + 1) * 512),
              s=Wkv[dk * 128:(dk + 1) * 128, :]: e.dma_start(d, s), inc="GWQ", incn=16)
        K_.op("gpsimd", lambda e, d=col(Wob, dk * 1024, (dk + 1) * 1024),
              s=Wo[dk * 128:(dk + 1) * 128, :]: e.dma_start(d, s), inc="GWQ", incn=16)
    gl_w = K_.c["GWQ"]
    K_.op("sync", lambda e: e.dma_start(alib[:], alin[:, :]), inc="LAL", incn=16)
    ld_al = K_.c["LAL"]

    mm_prev_b = 0
    ea_pj = {}
    SCALE = float(HDIM) ** -0.5
    for b in range(NB):
        gl_xa = None
        for dk in range(DK):
            gl_xa = K_.op("gpsimd", lambda e, d=col(xa, dk * 2048, (dk + 1) * 2048),
                          s=xin[dk * 128:(dk + 1) * 128, b * T:(b + 1) * T]: e.dma_start(d, s),
                          waits=[("MM", mm_prev_b)], inc="XA", incn=16)

        kidx = 0
        ev_vmemset = K_.op("vector", lambda e: e.memset(vext[:, :], 1.0),
                           waits=[("MM", mm_prev_b)], inc="EV")
        for hh in range(DK):
            for tt in range(4):
                kk = kidx; kidx += 1

                def qmm(e, hh=hh, tt=tt, kk=kk):
                    bank = P[4 + kk % 2]
                    for dk in range(DK):
                        ins = e.matmul(bank[:, 0:512],
                                       col(Wqb, dk * 1024 + hh * 128, dk * 1024 + hh * 128 + 128),
                                       col(xa, dk * 2048 + tt * 512, dk * 2048 + tt * 512 + 512),
                                       start=(dk == 0), stop=(dk == DK - 1))
                    return ins

                w = [("XA", gl_xa), ("GWQ", gl_w)]
                if kk >= 2:
                    w.append(("EA", ea_pj[kk - 2]))
                mm_p = K_.op("tensor", qmm, waits=w, inc="MM")
                ea_pj[kk] = K_.op("scalar",
                                  lambda e, hh=hh, tt=tt, kk=kk:
                                  e.activation(col(qoT, hh * 2048 + tt * 512, hh * 2048 + tt * 512 + 512),
                                               P[4 + kk % 2][:, 0:512], AF.Identity, scale=SCALE),
                                  waits=[("MM", mm_p)], inc="EA")
        for kc2 in range(2):
            for tt in range(4):
                kk = kidx; kidx += 1

                def kmm(e, kc2=kc2, tt=tt, kk=kk):
                    bank = P[4 + kk % 2]
                    for dk in range(DK):
                        ins = e.matmul(bank[:, 0:512],
                                       col(Wkvb, dk * 512 + kc2 * 128, dk * 512 + kc2 * 128 + 128),
                                       col(xa, dk * 2048 + tt * 512, dk * 2048 + tt * 512 + 512),
                                       start=(dk == 0), stop=(dk == DK - 1))
                    return ins

                w = [("XA", gl_xa), ("GWQ", gl_w)]
                if kk >= 2:
                    w.append(("EA", ea_pj[kk - 2]))
                mm_p = K_.op("tensor", kmm, waits=w, inc="MM")

                def kev(e, kc2=kc2, tt=tt, kk=kk):
                    pk = P[4 + kk % 2]
                    c0a = (2 * kc2) * 2048 + tt * 512
                    c0b = (2 * kc2 + 1) * 2048 + tt * 512
                    e.activation(kT[0:64, c0a:c0a + 512], pk[0:64, 0:512], AF.Identity)
                    e.activation(kT[64:128, c0a:c0a + 512], pk[0:64, 0:512], AF.Identity)
                    e.activation(kT[0:64, c0b:c0b + 512], pk[64:128, 0:512], AF.Identity)
                    return e.activation(kT[64:128, c0b:c0b + 512], pk[64:128, 0:512], AF.Identity)

                ea_pj[kk] = K_.op("scalar", kev, waits=[("MM", mm_p)], inc="EA", n_ins=4)
        for kv2 in range(2):
            for tc in range(16):
                kk = kidx; kidx += 1

                def vmm(e, kv2=kv2, tc=tc, kk=kk):
                    bank = P[4 + kk % 2]
                    for dk in range(DK):
                        ins = e.matmul(bank[:, 0:128],
                                       col(xa, dk * 2048 + tc * 128, dk * 2048 + tc * 128 + 128),
                                       col(Wkvb, dk * 512 + 256 + kv2 * 128, dk * 512 + 256 + kv2 * 128 + 128),
                                       start=(dk == 0), stop=(dk == DK - 1))
                    return ins

                w = [("XA", gl_xa), ("GWQ", gl_w)]
                if kk >= 2:
                    w.append(("EA", ea_pj[kk - 2]))
                mm_p = K_.op("tensor", vmm, waits=w, inc="MM")

                def vev(e, kv2=kv2, tc=tc, kk=kk):
                    e.activation(col(vext, (kv2 * 2 * 16 + tc) * 65, (kv2 * 2 * 16 + tc) * 65 + 64),
                                 P[4 + kk % 2][:, 0:64], AF.Identity)
                    return e.activation(col(vext, ((kv2 * 2 + 1) * 16 + tc) * 65,
                                            ((kv2 * 2 + 1) * 16 + tc) * 65 + 64),
                                        P[4 + kk % 2][:, 64:128], AF.Identity)

                w = [("MM", mm_p)]
                if kv2 == 0 and tc == 0:
                    w.append(("EV", ev_vmemset))
                ea_pj[kk] = K_.op("scalar", vev, waits=w, inc="EA", n_ins=2)
        ea_proj_last = ea_pj[kidx - 1]

        ev_obf_last = 0
        ea_oev = 0
        ev_recip = 0
        ea_es = {}
        mm_av = {}
        ev_obf = {}
        for H in range(16):
            kv, gq = H // 4, H % 4
            hh = 2 * kv + gq // 2
            ro = (gq % 2) * 64
            kc2 = kv // 2
            kro = (kv % 2) * 64
            for jq in range(4):
                mm_s = {}
                for tk in range(16):
                    def smm(e, H=H, jq=jq, tk=tk, kv=kv, hh=hh, ro=ro):
                        return e.matmul(P[tk % 2][:, 0:512],
                                        kT[ro:ro + 64, kv * 2048 + tk * 128: kv * 2048 + tk * 128 + 128],
                                        qoT[ro:ro + 64, hh * 2048 + jq * 512: hh * 2048 + jq * 512 + 512],
                                        start=True, stop=True)

                    w = [("EA", ea_proj_last if (H, jq, tk) == (0, 0, 0) else 0),
                         ("LAL", ld_al if (H, jq, tk) == (0, 0, 0) else 0)]
                    if tk >= 2:
                        w.append(("EA", ea_es[tk - 2]))
                    elif ea_es:
                        w.append(("EA", ea_es.get(14 + tk, 0)))
                    mm_s[tk] = K_.op("tensor", smm, waits=w, inc="MM")

                    w = [("MM", mm_s[tk])]
                    if tk >= 2:
                        w.append(("MM", mm_av.get(tk - 2, 0)))
                    elif mm_av:
                        w.append(("MM", mm_av.get(14 + tk, 0)))
                    ea_es[tk] = K_.op("scalar",
                                      lambda e, H=H, tk=tk:
                                      e.activation(es[tk % 2][:, :], P[tk % 2][:, 0:512], AF.Exp,
                                                   bias=alib[:, H * 16 + tk:H * 16 + tk + 1]),
                                      waits=w, inc="EA")

                    def avmm(e, kv=kv, tk=tk):
                        return e.matmul(P[2][0:65, 0:512],
                                        col(vext, (kv * 16 + tk) * 65, (kv * 16 + tk) * 65 + 65),
                                        es[tk % 2][:, :], start=(tk == 0), stop=(tk == 15))

                    w = [("EA", ea_es[tk])]
                    if tk == 0:
                        w += [("EV", ev_recip), ("EA", ea_oev)]
                    mm_av[tk] = K_.op("tensor", avmm, waits=w, inc="MM")

                ev_recip = K_.op("vector", lambda e: e.reciprocal(rlrow[:, :], P[2][64:65, 0:512]),
                                 waits=[("MM", mm_av[15])], inc="EV")
                ea_oev = K_.op("scalar", lambda e: e.activation(o_sb[0:64, :], P[2][0:64, 0:512],
                                                                AF.Identity),
                               waits=[("MM", mm_av[15]), ("EV", ev_obf_last)], inc="EA")
                mm_rl = K_.op("tensor", lambda e: e.matmul(P[3][0:64, 0:512], onesf[:, 0:64],
                                                           rlrow[:, :], start=True, stop=True),
                              waits=[("EV", ev_recip), ("EV", ev_obf_last)], inc="MM")
                ev_obf_last = K_.op("vector",
                                    lambda e, hh=hh, ro=ro, jq=jq:
                                    e.tensor_tensor(qoT[ro:ro + 64, hh * 2048 + jq * 512: hh * 2048 + jq * 512 + 512],
                                                    o_sb[0:64, :], P[3][0:64, 0:512], ALU.mult),
                                    waits=[("MM", mm_rl), ("EA", ea_oev)], inc="EV")
                ev_obf[(H, jq)] = ev_obf_last

        ea_rv = {}
        ev_rr = {}
        for tt in range(4):
            cols = (b * T + tt * 512, b * T + tt * 512 + 512)
            ev_rr_list = []
            for do in range(DK):
                k = tt * 8 + do

                def wmm(e, tt=tt, do=do, k=k):
                    bank = P[4 + k % 2]
                    for cc in range(DK):
                        ins = e.matmul(bank[:, 0:512],
                                       col(Wob, cc * 1024 + do * 128, cc * 1024 + do * 128 + 128),
                                       col(qoT, cc * 2048 + tt * 512, cc * 2048 + (tt + 1) * 512),
                                       start=(cc == 0), stop=(cc == DK - 1))
                    return ins

                w = [("EV", ev_obf[(15, tt)])]
                if k >= 2:
                    w.append(("EA", ea_rv[k - 2]))
                else:
                    w.append(("EA", ea_proj_last))
                mm_w = K_.op("tensor", wmm, waits=w, inc="MM")
                ea_rv[k] = K_.op("scalar",
                                 lambda e, k=k: e.activation(evb[k % 2][:, :],
                                                             P[4 + k % 2][:, 0:512], AF.Identity),
                                 waits=[("MM", mm_w), ("EV", ev_rr.get(k - 2, 0))], inc="EA")
                ev_rr[k] = ln.resid_rr(xin, cols, do, evb[k % 2][:, :], [("EA", ea_rv[k])])
                ev_rr_list.append(ev_rr[k])
            ln.tile(xout, cols, ev_rr_list)
        mm_prev_b = K_.c["MM"]
        print(f"seq@C_b{b}:", K_.seq)

    sc_.close()


def emit_stage_d(nc, K_, stack, wd, xin, xout, g):
    sd_ = ExitStack()
    sb = lambda n, s, d=BF16: sd_.enter_context(nc.sbuf_tensor(n, s, d))
    HFK = HF // 128
    xfd = sb("xfd", [128, DK * 1024])
    h1b = sb("h1b", [128, HFK * 1024])
    g1b = sb("g1b", [128, HFK * 1024])
    wbg = [sb(f"wbg{i}", [128, 4096]) for i in range(2)]
    wsm = [wbg[0][:, 0:1024], wbg[0][:, 1024:2048], wbg[1][:, 0:1024], wbg[1][:, 1024:2048]]
    a_s = [sb(f"a_s{i}", [128, 512], F32) for i in range(2)]
    s1t = sb("s1t", [128, HFK], F32)
    s2t = sb("s2t", [128, HFK], F32)
    sh1t = sb("sh1t", [128, HFK], F32)
    sh2t = sb("sh2t", [128, HFK], F32)
    s3t = sb("s3t", [128, DK], F32)

    P = g["P"]
    ln = g["ln"]
    w1T, w2T = wd["w1T"], wd["w2T"]
    h1T, h2T, w3T = wd["h1T"], wd["h2T"], wd["w3T"]
    s1d, s2d = wd["L1_ffn_w1_s"], wd["L1_ffn_w2_s"]
    sh1d, sh2d = wd["L1_ffn_h1_s"], wd["L1_ffn_h2_s"]
    s3d = wd["L1_ffn_w3_s"]

    for hk in range(HFK):
        for dst, srcd in [(s1t, s1d), (s2t, s2d), (sh1t, sh1d), (sh2t, sh2d)]:
            K_.op("sync", lambda e, d=dst[:, hk:hk + 1], s=bass.AP(srcd, hk * 128, [[1, 128], [1, 1]]):
                  e.dma_start(d, s), inc="LDS", incn=16)
    for do in range(DK):
        K_.op("sync", lambda e, d=s3t[:, do:do + 1], s=bass.AP(s3d, do * 128, [[1, 128], [1, 1]]):
              e.dma_start(d, s), inc="LDS", incn=16)
    ld_s = K_.c["LDS"]

    def wslice_ap(src, kdim, hk):
        return bass.AP(src, hk * 128, [[src.shape[1], 128], [128 * src.shape[1], kdim // 128], [1, 128]])

    mm_prev_st = 0
    mmread = {}
    for st in range(TC // 1024):
        c0 = st * 1024
        gl_xd = None
        for dk in range(DK):
            gl_xd = K_.op("gpsimd", lambda e, d=col(xfd, dk * 1024, (dk + 1) * 1024),
                          s=xin[dk * 128:(dk + 1) * 128, c0:c0 + 1024]: e.dma_start(d, s),
                          waits=[("MM", mm_prev_st)], inc="XD", incn=16)

        gw = {}
        ea_as = {}
        ev_h1 = {}
        for hk in range(HFK):
            par = hk % 2
            gw[("w1", hk)] = K_.op("gpsimd", lambda e, hk=hk, par=par:
                                   e.dma_start(wsm[par].rearrange("p (dk c) -> p dk c", c=128),
                                               wslice_ap(w1T, D, hk)),
                                   waits=[("MM", mmread.get(("w1", par), 0)),
                                          ("MM", mmread.get(("W3s", 0), 0))], inc=f"D1{par}", incn=16)
            gw[("w2", hk)] = K_.op("gpsimd", lambda e, hk=hk, par=par:
                                   e.dma_start(wsm[2 + par].rearrange("p (dk c) -> p dk c", c=128),
                                               wslice_ap(w2T, D, hk)),
                                   waits=[("MM", mmread.get(("w2", par), 0)),
                                          ("MM", mmread.get(("W3s", 1), 0))], inc=f"D2{par}", incn=16)
            for tt in range(2):
                def l1mm(e, hk=hk, tt=tt, par=par):
                    pa, pb = P[par * 2], P[par * 2 + 1]
                    for dk in range(DK):
                        e.matmul(pa[:, 0:512], col(wsm[par], dk * 128, (dk + 1) * 128),
                                 col(xfd, dk * 1024 + tt * 512, dk * 1024 + tt * 512 + 512),
                                 start=(dk == 0), stop=(dk == DK - 1))
                    for dk in range(DK):
                        ins = e.matmul(pb[:, 0:512], col(wsm[2 + par], dk * 128, (dk + 1) * 128),
                                       col(xfd, dk * 1024 + tt * 512, dk * 1024 + tt * 512 + 512),
                                       start=(dk == 0), stop=(dk == DK - 1))
                    return ins

                prevu = (hk, 0) if tt == 1 else (hk - 2, 1)
                w = [(f"D1{par}", gw[("w1", hk)]), (f"D2{par}", gw[("w2", hk)]), ("XD", gl_xd),
                     ("EV", ev_h1.get(prevu, 0)), ("EA", ea_as.get(prevu, 0))]
                if hk < 2:
                    w.append(("EV", ln.ev_out_last))
                mm1 = K_.op("tensor", l1mm, waits=w, inc="MM")
                mmread[("w1", par)] = mm1
                mmread[("w2", par)] = mm1
                ea_as[(hk, tt)] = K_.op("scalar",
                                        lambda e, hk=hk, tt=tt, par=par:
                                        e.activation(a_s[par][:, :], P[par * 2][:, 0:512], AF.Silu,
                                                     scale=s1t[:, hk:hk + 1]),
                                        waits=[("MM", mm1), ("LDS", ld_s),
                                               ("EV", ev_h1.get(prevu, 0))], inc="EA")
                ev_h1[(hk, tt)] = K_.op("vector",
                                        lambda e, hk=hk, tt=tt, par=par:
                                        e.scalar_tensor_tensor(
                                            col(h1b, hk * 1024 + tt * 512, hk * 1024 + tt * 512 + 512),
                                            P[par * 2 + 1][:, 0:512], s2t[:, hk:hk + 1],
                                            a_s[par][:, :], ALU.mult, ALU.mult),
                                        waits=[("EA", ea_as[(hk, tt)]),
                                               ("MM", mmread.get(("h1r", hk), 0))], inc="EV")

        last_l1 = max(mmread[("w1", 0)], mmread[("w1", 1)])
        mmread[("H1s", 0)] = last_l1
        mmread[("H1s", 1)] = last_l1
        def big_layer(name, srcbuf, wT, kdim, nout, ev_src, evac_fn, tts=(0, 1)):
            last_ev = {}
            last_key = None
            for hk in range(nout):
                par = hk % 2
                gw[(name, hk)] = K_.op("gpsimd", lambda e, hk=hk, par=par, wT=wT, kdim=kdim:
                                       e.dma_start(wbg[par][:, 0:(kdim // 128) * 128]
                                                   .rearrange("p (kk c) -> p kk c", c=128),
                                                   wslice_ap(wT, kdim, hk)),
                                       waits=[("MM", mmread.get((name + "s", par), 0))],
                                       inc=f"DH{par}", incn=16)
                for tt in tts:
                    kk2 = hk * 2 + tt

                    def lmm(e, hk=hk, tt=tt, par=par, kdim=kdim, srcbuf=srcbuf, kk2=kk2):
                        bank = P[4 + kk2 % 2]
                        for kk in range(kdim // 128):
                            ins = e.matmul(bank[:, 0:512], col(wbg[par], kk * 128, (kk + 1) * 128),
                                           col(srcbuf, kk * 1024 + tt * 512, kk * 1024 + tt * 512 + 512),
                                           start=(kk == 0), stop=(kk == kdim // 128 - 1))
                        return ins

                    w = [(f"DH{par}", gw[(name, hk)]), ev_src]
                    if kk2 - 2 in last_ev:
                        w.append(last_ev[kk2 - 2])
                    mml = K_.op("tensor", lmm, waits=w, inc="MM")
                    mmread[(name + "s", par)] = mml
                    last_ev[kk2] = evac_fn(mml, hk, tt, kk2)
                    last_key = kk2
            return last_ev[last_key]

        def act_evac(dstbuf, scal_t, func):
            def f(mml, hk, tt, kk2):
                cnt = K_.op("scalar",
                            lambda e, hk=hk, tt=tt, kk2=kk2:
                            e.activation(col(dstbuf, hk * 1024 + tt * 512, hk * 1024 + tt * 512 + 512),
                                         P[4 + kk2 % 2][:, 0:512], func, scale=scal_t[:, hk:hk + 1]),
                            waits=[("MM", mml), ("LDS", ld_s)], inc="EA")
                return ("EA", cnt)
            return f

        ea_g1_last = big_layer("H1", h1b, h1T, HF, HFK,
                               ("EV", ev_h1[(HFK - 1, 1)]), act_evac(g1b, sh1t, AF.Gelu))
        for p_ in range(2):
            mmread[("H2s", p_)] = mmread[("H1s", p_)]
        ea_g2_last = big_layer("H2", g1b, h2T, HF, HFK,
                               ea_g1_last, act_evac(h1b, sh2t, AF.Gelu))
        for p_ in range(2):
            mmread[("W3s", p_)] = mmread[("H2s", p_)]

        rrlists = {0: [], 1: []}

        def l4_evac(mml, do, tt, kk2):
            cols = (c0 + tt * 512, c0 + tt * 512 + 512)
            kpar = (ln.k_global + tt * DK + do) % 2
            ld = K_.op("sync", lambda e, do=do, cols=cols, kpar=kpar:
                       e.dma_start(g["xtb"][kpar][:, :],
                                   xin[do * 128:do * 128 + 128, cols[0]:cols[1]]),
                       waits=[("EV", ln.ev_xtb_reader.get(kpar, 0))], inc=f"LX{kpar}", incn=16)
            ev = K_.op("vector",
                       lambda e, do=do, tt=tt, kpar=kpar, kk2=kk2:
                       e.scalar_tensor_tensor(col(g["rrf"], do * 512, (do + 1) * 512),
                                              P[4 + kk2 % 2][:, 0:512], s3t[:, do:do + 1],
                                              g["xtb"][kpar][:, :], ALU.mult, ALU.add),
                       waits=[("MM", mml), (f"LX{kpar}", ld), ("EA", ln.last_ea_sq[do]),
                              ("LDS", ld_s)], inc="EV")
            ln.ev_xtb_reader[kpar] = ev
            rrlists[tt].append((do, ev))
            return ("EV", ev)

        for tt in range(2):
            rrlists[tt] = []
            big_layer("W3", h1b, w3T, HF, DK, ea_g2_last, l4_evac, tts=(tt,))
            cols = (c0 + tt * 512, c0 + tt * 512 + 512)
            lst = [ev for do, ev in sorted(rrlists[tt])]
            ln.tile(xout, cols, lst)
        mm_prev_st = K_.c["MM"]
        print(f"seq@D_st{st}:", K_.seq)

    sd_.close()

def build_program(stages=("A", "B", "C", "D")):
    nc = bass.Bass("TRN2", target_bir_lowering=False)
    stack = ExitStack()
    K_ = K(nc, stack)

    xT = nc.dram_tensor("xT", [D, TC], F32, kind="ExternalInput")
    ident_in = nc.dram_tensor("ident_in", [128, 128], BF16, kind="ExternalInput")

    wd = {}

    def din(name, shape):
        wd[name] = nc.dram_tensor(name, list(shape), F32, kind="ExternalInput")

    if "A" in stages:
        for n, s in [("L0_attn_w_in", (D, 3 * D)), ("L0_attn_b_in", (3 * D,)),
                     ("L0_attn_w_conv", (3 * D, 1, 3)), ("L0_attn_b_conv", (3 * D,)),
                     ("L0_attn_w1", (3, 64)),
                     ("L0_attn_b1", (64,)), ("L0_attn_w2", (64, 64)), ("L0_attn_b2", (64,)),
                     ("L0_attn_w3", (64, D)), ("L0_attn_filt_bias", (D,)),
                     ("L0_attn_w_out", (D, D)), ("L0_attn_b_out", (D,))]:
            din(n, s)
    if "C" in stages:
        for n, s in [("L1_attn_Wq", (D, D)), ("L1_attn_Wkv", (D, 512)),
                     ("L1_attn_Wo", (D, D)), ("alibi_in", (128, 256))]:
            din(n, s)
    if "D" in stages:
        for n, s in [("w1T", (D, HF)), ("w2T", (D, HF)), ("h1T", (HF, HF)),
                     ("h2T", (HF, HF)), ("w3T", (HF, D)),
                     ("L1_ffn_w1_s", (HF,)), ("L1_ffn_w2_s", (HF,)),
                     ("L1_ffn_h1_s", (HF,)), ("L1_ffn_h2_s", (HF,)), ("L1_ffn_w3_s", (D,))]:
            din(n, s)
    if "B" in stages:
        for n, s in [("L0_ffn_gate_b", (8,)),
                     ("L0_ffn_w1", (8, D, 1408)), ("L0_ffn_b1", (8, 1408)),
                     ("L0_ffn_w3", (8, D, 1408)), ("L0_ffn_b3", (8, 1408)),
                     ("L0_ffn_w2", (8, 1408, D)), ("L0_ffn_b2", (8, D)),
                     ("L0_ffn_sw1", (D, 1408)), ("L0_ffn_sb1", (1408,)),
                     ("L0_ffn_sw3", (D, 1408)), ("L0_ffn_sb3", (1408,)),
                     ("L0_ffn_sw2", (1408, D)), ("L0_ffn_sb2", (D,)),
                     ("gate_wT", (D, 8)), ("lt_bc_in", (128, 64)),
                     ("identf_in", (128, 128)), ("esel_in", (8, 1024))]:
            din(n, s)

    order = [s for s in "ABCD" if s in stages]
    stage_io = {}
    prev = xT
    for i, s in enumerate(order):
        out = (nc.dram_tensor("outT", [D, TC], F32, kind="ExternalOutput")
               if i == len(order) - 1 else nc.dram_tensor(f"xs_{s}", [D, TC], F32, kind="Internal"))
        stage_io[s] = (prev, out)
        prev = out
    hpad = nc.dram_tensor("hpad_r1", [1280], F32, kind="Internal")

    g = {}
    g["ident"] = stack.enter_context(nc.sbuf_tensor("ident", [128, 128], BF16))
    g["onesb"] = stack.enter_context(nc.sbuf_tensor("onesb", [128, 1], BF16))
    g["onesf"] = stack.enter_context(nc.sbuf_tensor("onesf", [1, 128], F32))
    g["lnrow"] = stack.enter_context(nc.sbuf_tensor("lnrow", [1, 2048], F32))
    g["eps1"] = stack.enter_context(nc.sbuf_tensor("eps1", [1, 1], F32))
    g["rrf"] = stack.enter_context(nc.sbuf_tensor("rrf", [128, 8 * 512], F32))
    g["rrb"] = [stack.enter_context(nc.sbuf_tensor(f"rrb{i}", [128, 512], BF16)) for i in range(2)]
    g["sqb"] = [stack.enter_context(nc.sbuf_tensor(f"sqb{i}", [128, 512], BF16)) for i in range(2)]
    g["t2b"] = stack.enter_context(nc.sbuf_tensor("t2b", [128, 512], F32))
    g["obuf"] = [stack.enter_context(nc.sbuf_tensor(f"obuf{i}", [128, 512], F32)) for i in range(2)]
    g["xtb"] = [stack.enter_context(nc.sbuf_tensor(f"xtb{i}", [128, 512], F32)) for i in range(2)]
    g["P"] = [stack.enter_context(nc.psum_tensor(f"p{i}", [128, 512], F32)) for i in range(6)]
    g["PT"] = stack.enter_context(nc.psum_tensor("pt", [128, 512], BF16))
    g["PST"] = stack.enter_context(nc.psum_tensor("pst", [33, 512], F32))
    g["pab0"], g["pab1"] = g["P"][0], g["P"][1]

    ld_id = K_.op("sync", lambda e: e.dma_start(g["ident"][:], ident_in[:, :]), inc="LID", incn=16)

    def initones(e):
        e.memset(g["onesb"][:], 1.0)
        e.memset(g["eps1"][:], EPS)
        return e.memset(g["onesf"][:], 1.0)

    g["ev_init"] = K_.op("vector", initones, waits=[("LID", ld_id)], inc="EV", n_ins=3)
    g["ln"] = LNTail(K_, g)

    if "A" in stages:
        emit_stage_a(nc, K_, stack, wd, *stage_io["A"], hpad, g)
    if "B" in stages:
        if "A" in stages:
            stage_barrier(K_)
        emit_stage_b(nc, K_, stack, wd, *stage_io["B"], g)
    if "C" in stages:
        if len(stages) > 1:
            stage_barrier(K_)
        emit_stage_c(nc, K_, stack, wd, *stage_io["C"], g)
    if "D" in stages:
        if len(stages) > 1:
            stage_barrier(K_)
        emit_stage_d(nc, K_, stack, wd, *stage_io["D"], g)

    import os as _os
    _tr = int(_os.environ.get("KTRUNC", "0"))
    if _tr:
        K_.truncate(_tr)
    K_.op("sync", lambda e: e.nop(), waits=[(s, K_.c[s]) for s in K_.c])
    print("total ops seq:", K_.seq)

    block = stack.enter_context(nc.Block())
    K_.emit(block)
    stack.close()
    return nc, ["xT", "ident_in"] + sorted(wd.keys())


def shard_inputs(inputs, core, names, xkey="L0_attn_x"):
    x = np.asarray(inputs[xkey])
    sh = x[core * NB:(core + 1) * NB]
    xT = np.ascontiguousarray(sh.transpose(2, 0, 1).reshape(D, TC))
    m = {"xT": xT, "ident_in": np.eye(128, dtype=BF)}
    derived = {}
    if "w1T" in names:
        derived["w1T"] = np.ascontiguousarray(np.asarray(inputs["L1_ffn_w1_w"], np.float32).T)
        derived["w2T"] = np.ascontiguousarray(np.asarray(inputs["L1_ffn_w2_w"], np.float32).T)
        derived["h1T"] = np.ascontiguousarray(np.asarray(inputs["L1_ffn_h1_w"], np.float32).T)
        derived["h2T"] = np.ascontiguousarray(np.asarray(inputs["L1_ffn_h2_w"], np.float32).T)
        derived["w3T"] = np.ascontiguousarray(np.asarray(inputs["L1_ffn_w3_w"], np.float32).T)
    if "alibi_in" in names:
        al = np.zeros((128, 256), np.float32)
        p = np.arange(128)
        for H in range(16):
            slope = (2.0 ** 8) ** (-(H + 1) / 16.0)
            for kc in range(16):
                al[:, H * 16 + kc] = slope * (128 * kc + p - 2047.0) - 30.0
        derived["alibi_in"] = al
    if "gate_wT" in names:
        derived["gate_wT"] = np.ascontiguousarray(
            np.asarray(inputs["L0_ffn_gate_w"], dtype=np.float32).T)
        lt = np.zeros((128, 64), np.float32)
        for j in range(8):
            for e in range(8):
                if j < e:
                    lt[:, 8 * j + e] = 1.0
        derived["lt_bc_in"] = lt
        derived["identf_in"] = np.eye(128, dtype=np.float32)
        es = np.zeros((8, 1024), np.float32)
        for e in range(8):
            es[e, e * 128:(e + 1) * 128] = 1.0
        derived["esel_in"] = es
    for k in names:
        if k in ("xT", "ident_in"):
            continue
        if k in derived:
            m[k] = derived[k]
        else:
            m[k] = np.ascontiguousarray(np.asarray(inputs[k], dtype=np.float32))
    return m


def run_stages(inputs, stages=("A",), xkey="L0_attn_x"):
    nc, names = build_program(stages)
    in_maps = [shard_inputs(inputs, core, names, xkey) for core in range(8)]
    res = run_bass_kernel_spmd(nc, in_maps, core_ids=list(range(8)))
    outs = []
    for core in range(8):
        oT = res.results[core]["outT"]
        outs.append(oT.reshape(D, NB, T).transpose(1, 2, 0))
    return np.concatenate(outs, axis=0).astype(np.float32)


FUSED = False


def kernel(**inputs):
    if FUSED:
        return run_stages(inputs, ("A", "B", "C", "D"))
    cur = None
    for s in "ABCD":
        nc, names = build_program((s,))
        in_maps = []
        for core in range(8):
            m = shard_inputs(inputs, core, names)
            if cur is not None:
                m["xT"] = cur[core]
            in_maps.append(m)
        res = run_bass_kernel_spmd(nc, in_maps, core_ids=list(range(8)))
        cur = [np.ascontiguousarray(res.results[c]["outT"]) for c in range(8)]
    outs = [cur[c].reshape(D, NB, T).transpose(1, 2, 0) for c in range(8)]
    return np.concatenate(outs, axis=0).astype(np.float32)
